# version 06
# baseline (speedup 1.0000x reference)
import numpy as np
import ml_dtypes
import concourse.bass as bass
import concourse.mybir as mybir
from concourse.bass_utils import run_bass_kernel_spmd

F32 = mybir.dt.float32
BF16 = mybir.dt.bfloat16
AX = mybir.AxisListType
ALU = mybir.AluOpType
ACT = mybir.ActivationFunctionType

S, D, NH, HD = 8192, 2048, 16, 128
C = 512
NCORE = 8
TOK = 1024
HALO = 128
NOWN = 1152
QT = 8
FC = 16
FF = 8192
NSLAB = 64
ROPE = 64
R = 16
WIN = 128
MAGIC = 12582912.0
SCL = float(1.0 / np.sqrt(HD))
BF2 = ml_dtypes.bfloat16
TOKA, TOKB = 768, 256


def _esz(dt):
    s = str(dt)
    if "32" in s:
        return 4
    if "8" in s:
        return 1
    return 2


_VRW = {
    "tensor_tensor": ((0,), (1, 2)),
    "tensor_add": ((0,), (1, 2)),
    "tensor_sub": ((0,), (1, 2)),
    "tensor_max": ((0,), (1, 2)),
    "tensor_copy": ((0,), (1,)),
    "tensor_scalar": ((0,), (1, 2, 3)),
    "scalar_tensor_tensor": ((0,), (1, 2, 3)),
    "reduce_sum": ((0,), (1,)),
    "reduce_max": ((0,), (1,)),
    "reciprocal": ((0,), (1,)),
    "memset": ((0,), ()),
}


def _ap_rng(ap, rmap):
    try:
        h = ap.tensor
        reg, base = rmap.get(h.name, (h.name, 0))
        reg = str(ap.space) + reg
        p0 = ap.base_partition()
        pn = ap.partition_size()
        esz = _esz(ap.dtype)
        row = (int(h.shape[-1]) * _esz(h.dtype)) // esz
        off = ap.offset
        if ap.ndim == 2 and off // row == p0:
            b0 = base + (off % row) * esz
            b1 = b0 + ap.free_nbytes()
        else:
            b0 = base
            b1 = base + row * esz
        return (reg, p0, p0 + pn, b0, b1)
    except Exception:
        return ("?", 0, 128, 0, 1 << 30)


class _VProxy:
    def __init__(self, v, rmap, hz, stats):
        self._v = v
        self._rmap = rmap
        self._hz = hz
        self._stats = stats
        self._prev = []

    def __getattr__(self, name):
        real = getattr(self._v, name)
        spec = _VRW.get(name)
        if spec is None:
            return real

        def call(*args, **kwargs):
            aps = [(i, x) for i, x in enumerate(args) if type(x).__name__ == "AP"]
            writes = [x for i, x in aps if i in spec[0]]
            reads = [x for i, x in aps if i in spec[1]]
            ao = kwargs.get("accum_out")
            if ao is not None:
                writes = writes + [ao]
            if self._prev:
                cur = [_ap_rng(x, self._rmap) for x in reads + writes]
                hit = any(a[0] == b[0] and a[1] < b[2] and b[1] < a[2]
                          and a[3] < b[4] and b[3] < a[4]
                          for a in self._prev for b in cur)
                if hit:
                    self._v.tensor_copy(self._hz[:, 0:4], self._hz[:, 4:8])
                    self._stats[0] += 1
            ins = real(*args, **kwargs)
            self._prev = [_ap_rng(x, self._rmap) for x in writes]
            return ins
        return call


class Prog:
    ENGS = ("sync", "gpsimd", "vector", "scalar", "tensor")

    def __init__(self, nc):
        self.nc = nc
        self.streams = {e: [] for e in self.ENGS}
        self.sems = {}
        self.counts = {}
        self.regmap = {}
        self.hz_ap = None
        self.hz_stats = [0]

    def sem(self, name):
        if name not in self.sems:
            self.sems[name] = self.nc.alloc_semaphore(name)
            self.counts[name] = 0
        return self.sems[name]

    def wait(self, eng, name, val=None):
        sem = self.sem(name)
        v = self.counts[name] if val is None else val
        if v > 0:
            self.streams[eng].append(("wait", sem, v))

    def op(self, eng, fn, sig=None, n=1):
        sem = None
        if sig is not None:
            sem = self.sem(sig)
            self.counts[sig] += n
        self.streams[eng].append(("op", fn, sem, n))
        return self.counts[sig] if sig else None

    def dma(self, eng, out, in_, sig):
        sem = self.sem(sig)
        self.counts[sig] += 16
        self.streams[eng].append(("dma", out, in_, sem))
        return self.counts[sig]

    def val(self, name):
        self.sem(name)
        return self.counts[name]

    def emit(self, block):
        def mk(eng_name):
            def body(engobj):
                if eng_name == "vector" and self.hz_ap is not None:
                    engobj = _VProxy(engobj, self.regmap, self.hz_ap,
                                     self.hz_stats)
                for item in self.streams[eng_name]:
                    if item[0] == "wait":
                        engobj.wait_ge(item[1], item[2])
                    elif item[0] == "dma":
                        engobj.dma_start(out=item[1], in_=item[2]).then_inc(item[3], 16)
                    else:
                        ins = item[1](engobj)
                        if item[2] is not None:
                            if ins is None:
                                ins = engobj.nop()
                            ins.then_inc(item[2], item[3])
            return body
        block.sync(mk("sync"))
        block.gpsimd(mk("gpsimd"))
        block.vector(mk("vector"))
        block.scalar(mk("scalar"))
        block.tensor(mk("tensor"))


def build_program(debug=()):
    nc = bass.Bass("TRN2", target_bir_lowering=False, detect_race_conditions=False)
    p = Prog(nc)
    T, V, SC, SY, G = "tensor", "vector", "scalar", "sync", "gpsimd"

    def din(name, shape, dtype=F32):
        return nc.dram_tensor(name, list(shape), dtype, kind="ExternalInput").ap()

    x_full = din("x_full", [S, D])
    x_own = din("x_own", [NOWN, D])
    Wq_d = din("Wq", [D, D]); Wlk_d = din("Wlk", [D, D]); Wlv_d = din("Wlv", [D, D])
    Wo_d = din("Wo", [D, D])
    Wk_d = din("Wk", [D, HD]); Wv_d = din("Wv", [D, HD])
    Wc_row = din("Wc_row", [1, D])
    bq_cols = din("bq_cols", [128, FC])
    blk_cols = din("blk_cols", [128, FC])
    blv_row_d = din("blv_row", [1, D], BF16)
    bo_row_d = din("bo_row", [1, D], BF16)
    bv_row_d = din("bv_row", [1, HD], BF16)
    bc_d = din("bc", [1, 1])
    bk_col_d = din("bk_col", [HD, 1])
    qn_col_d = din("qn_col", [HD, 1]); kn_col_d = din("kn_col", [HD, 1])
    qn_lo_d = din("qn_lo", [32, 1]); qn_mid_d = din("qn_mid", [32, 1])
    kn_lo_d = din("kn_lo", [32, 1]); kn_mid_d = din("kn_mid", [32, 1])
    sink_k_d = din("sink_k", [NH, HD])
    sv_row_d = din("sv_row", [1, NH * HD], BF16)
    wg_d = din("wg", [FF, D]); wu_d = din("wu", [FF, D]); wd_d = din("wd", [D, FF])
    ident_bf_d = din("ident_bf", [128, 128], BF16)
    ident_f_d = din("ident_f", [128, 128])
    ones_row_d = din("ones_row", [1, 128])
    ones_bfrow_d = din("ones_bfrow", [1, 128], BF16)
    onescol_bf_d = din("onescol_bf", [128, 1], BF16)
    m01_d = din("m01_in", [128, 8])
    cos_own_d = din("cos_own", [64, NOWN]); sin_own_d = din("sin_own", [64, NOWN])
    cos_blk_d = din("cos_blk", [64, C]); sin_blk_d = din("sin_blk", [64, C])
    mask_loc_d = din("mask_loc", [2 * 128, 256], BF16)
    mask_cmp_d = din("mask_cmp", [QT * 128, C], BF16)

    out_d = nc.dram_tensor("out", [TOK, D], F32, kind="ExternalOutput").ap()
    resid_d = nc.dram_tensor("resid_scr", [TOK, D], F32, kind="Internal").ap()
    xq_d = nc.dram_tensor("xq_scr", [D, TOK], BF16, kind="Internal").ap()
    recips_d = nc.dram_tensor("recips_scr", [1, TOK], F32, kind="Internal").ap()
    s1_d = nc.dram_tensor("s1_scr", [128 * 384], F32, kind="Internal").ap()
    l2_d = nc.dram_tensor("l2_scr", [3, 16, 1024], F32, kind="Internal").ap()
    slabg_d = nc.dram_tensor("slabg_scr", [NSLAB * 128, FC * 128], BF16, kind="Internal").ap()
    slabu_d = nc.dram_tensor("slabu_scr", [NSLAB * 128, FC * 128], BF16, kind="Internal").ap()
    slabd_d = nc.dram_tensor("slabd_scr", [NSLAB * 128, FC * 128], BF16, kind="Internal").ap()

    dbg = {}

    def dout(nm, shape, dtype=F32):
        if nm in debug:
            dbg[nm] = nc.dram_tensor("dbg_" + nm, list(shape), dtype,
                                     kind="ExternalOutput").ap()
        return dbg.get(nm)

    dout("entT", [128, FC * C]); dout("ckT", [128, C], BF16)
    dout("cv", [128, 4 * HD], BF16)
    dout("xnT", [128, FC * NOWN], BF16); dout("mergedT", [128, NH * TOK], BF16)
    dout("wmean", [1, 16]); dout("qk", [128, TOK + NOWN], BF16)
    dout("resid", [TOK, D]); dout("xq", [128, FC * 128], BF16)
    dout("recips", [1, TOK])
    dout("slabg", [NSLAB * 128, FC * 128], BF16)
    dout("slabd", [NSLAB * 128, FC * 128], BF16)
    dout("hsb", [128, NSLAB * TOKA], BF16); dout("hq", [128, NSLAB * TOKA], BF16)
    dout("hrow", [1, TOKA]); dout("hsig", [1, TOKA], BF16)
    dout("hdiag", [2, TOKA]); dout("hmaxd", [128, TOK])
    dout("own0", [128, D + 16]); dout("s1d", [128, 384]); dout("l2d", [48, 1024])

    def sb(name, shape, dtype=F32):
        return nc.alloc_sbuf_tensor(name, list(shape), dtype).ap()

    R1 = 131072
    r1b, _ = nc.bump_sbuf(R1)

    def rg1(name, shape, dtype, off):
        elem = {F32: 4, BF16: 2}[dtype]
        assert off % 4 == 0 and off + shape[1] * elem <= R1, (name, off)
        p.regmap[name] = ("R1", off)
        return nc.alloc_sbuf_tensor_at(name, list(shape), dtype, offset=r1b + off).ap()

    xnT = rg1("xnT", [128, FC * NOWN], BF16, 0)
    entf = rg1("entf", [128, FC * C], F32, 36864)
    entTn = rg1("entTn", [128, FC * C], BF16, 36864)
    mergedT = rg1("mergedT", [128, NH * TOK], BF16, 36864)
    p1t = [rg1(f"p1t{i}", [128, 4096], F32, 69632 + 16384 * i) for i in range(2)]
    m_cmp = rg1("m_cmp", [128, QT * C], BF16, 102400)
    m_loc = rg1("m_loc", [128, 2 * 256], BF16, 110592)
    coskb = rg1("coskb", [64, C], F32, 111616)
    sinkb = rg1("sinkb", [64, C], F32, 113664)
    cosb = rg1("cosb", [64, NOWN], F32, 115712)
    sinb = rg1("sinb", [64, NOWN], F32, 120320)
    q_sb = rg1("q_sb", [128, TOK], BF16, 124928)
    k_sb = rg1("k_sb", [128, NOWN], BF16, 126976)
    h_sb = rg1("h_sb", [128, NSLAB * TOKA], BF16, 0)
    hb_sb = rg1("hb_sb", [128, NSLAB * TOKB], BF16, 0)
    xqa = rg1("xqa", [128, FC * TOKA], BF16, 98304)
    xqb = rg1("xqb", [128, FC * TOKB], BF16, 98304)
    hmax = rg1("hmax", [128, TOK], F32, 126976)
    tscr = rg1("tscr", [128, 1024], BF16, 124928)

    R4 = 26112
    r4b, _ = nc.bump_sbuf(R4)

    def rg4(name, shape, dtype, off):
        elem = {F32: 4, BF16: 2}[dtype]
        assert off % 4 == 0 and off + shape[1] * elem <= R4, (name, off)
        p.regmap[name] = ("R4", off)
        return nc.alloc_sbuf_tensor_at(name, list(shape), dtype, offset=r4b + off).ap()

    xt_sb = [rg4(f"xt{i}", [128, D + 2], F32, 8448 * i) for i in range(2)]
    wcb = rg4("wcb", [128, D + 1], F32, 16896)
    wk_sb = rg4("wk_sb", [128, FC * HD], BF16, 0)
    wv_sb = rg4("wv_sb", [128, FC * HD], BF16, 4096)
    wcol = [rg4(f"wcol{i}", [128, FC * 128], BF16, 4096 * i) for i in range(6)]
    wo_buf = rg4("wo_buf", [128, FC * 512], BF16, 0)
    wnat = [rg4(f"wnat{i}", [128, D], F32, 8192 * i) for i in range(2)]
    tern_sb = [rg4("ternA", [128, D], BF16, 16384),
               rg4("ternB", [128, D], BF16, 20480)]
    slab_sb = [rg4("slabA", [128, FC * 128], BF16, 16384),
               rg4("slabB", [128, FC * 128], BF16, 20480)]

    R5 = 28672
    r5b, _ = nc.bump_sbuf(R5)

    def rg5(name, shape, dtype, off):
        elem = {F32: 4, BF16: 2}[dtype]
        assert off % 4 == 0 and off + shape[1] * elem <= R5, (name, off)
        p.regmap[name] = ("R5", off)
        return nc.alloc_sbuf_tensor_at(name, list(shape), dtype, offset=r5b + off).ap()

    rows_bf = rg5("rows_bf", [1, 3 * D + HD], BF16, 0)
    xqstage = rg5("xqstage", [128, FC * 128], BF16, 12800)
    v_h = [rg5(f"v_h{i}", [128, 9 * HD], BF16, 12800 + 2304 * i) for i in range(2)]
    attLb = rg5("attLb", [128, 256], BF16, 17408)
    attCb = rg5("attCb", [128, 516], BF16, 17920)
    attT = rg5("attT", [128, 768], BF16, 18976)
    bcq = rg5("bcq", [128, TOK], BF16, 20992)
    bck = rg5("bck", [128, NOWN], BF16, 23040)
    ent_m = rg5("ent_m", [1, C], F32, 20992)
    ent_s = rg5("ent_s", [1, C], F32, 23040)
    bgu = rg5("bgu", [128, 2 * TOK], F32, 0)
    bxh = rg5("bxh", [128, TOK], F32, 8192)
    res_sb = [rg5(f"res_sb{i}", [128, 512], F32, 12288 + 2048 * i) for i in range(2)]
    outt = [rg5(f"outt{i}", [128, 512], F32, 16384 + 2048 * i) for i in range(2)]
    xres = [rg5(f"xres{i}", [128, 512], F32, 20480 + 2048 * i) for i in range(2)]
    rsh_row = rg5("rsh_row", [1, TOK], F32, 24576)
    ckf = rg5("ckf", [128, 512], F32, 16384)

    scr2 = sb("scr2", [128, 1024], F32)
    qsq = sb("qsq", [128, NOWN], BF16)
    trash = qsq
    ckT = sb("ckT", [128, C], BF16)
    cv_sb = sb("cv_sb", [128, 4 * HD], BF16)
    sinkT = sb("sinkT", [128, NH], BF16)
    s1sb = sb("s1sb", [128, 384], F32)
    shared_rows = sb("shared_rows", [1, 1024], F32)
    rowA = shared_rows[0:1, 0:512]
    rowB = shared_rows
    rq_bf = sb("rq_bf", [1, NOWN], BF16)
    stat = sb("stat", [128, 16], F32)
    col1 = sb("col1", [128, 8], F32)
    acol = [sb(f"acol{i}", [128, 8], F32) for i in range(2)]
    mcol = sb("mcol", [128, 12], F32)
    msc = sb("msc", [1, 16], F32)
    id_f = sb("id_f", [128, 128], F32)
    id_bf = sb("id_bf", [128, 128], BF16)
    onesr = sb("onesr", [1, 128])
    onesbr = sb("onesbr", [1, 128], BF16)
    onescb = sb("onescb", [128, 1], BF16)
    m01 = sb("m01", [128, 8], F32)
    bqc = sb("bqc", [128, FC], F32)
    blkc = sb("blkc", [128, FC], F32)
    bkc = sb("bkc", [128, 1], F32)
    gq_sb = sb("gq_sb", [128, 1], F32); gk_sb = sb("gk_sb", [128, 1], F32)
    gql = sb("gql", [32, 1], F32); gqm = sb("gqm", [32, 1], F32)
    gkl = sb("gkl", [32, 1], F32); gkm = sb("gkm", [32, 1], F32)
    srow_bf = sb("srow_bf", [1, 128], BF16)
    halfn = sb("halfn", [128, 1], F32)
    halfp = sb("halfp", [128, 1], F32)
    p.hz_ap = sb("hzb", [128, 8], F32)
    attL = scr2[:, 0:256]
    attC = scr2[:, 256:772]
    l2buf = scr2[0:16, 0:1024]

    PS = [nc.alloc_psum_tensor(f"PS{i}", [128, 512], F32).ap() for i in range(6)]
    PB = [nc.alloc_psum_tensor(f"PB{i}", [128, 1024], BF16).ap() for i in range(2)]
    PBf = [b.bitcast(F32) for b in PB]

    for dst, src in [
        (id_bf[:], ident_bf_d), (id_f[:], ident_f_d),
        (onesr[:], ones_row_d), (onesbr[:], ones_bfrow_d), (onescb[:], onescol_bf_d),
        (m01[:], m01_d),
        (cosb[:], cos_own_d), (sinb[:], sin_own_d),
        (coskb[:], cos_blk_d), (sinkb[:], sin_blk_d),
        (m_loc.rearrange("p (a n) -> p a n", a=2), mask_loc_d.rearrange("(a p) n -> p a n", a=2)),
        (m_cmp.rearrange("p (a n) -> p a n", a=QT), mask_cmp_d.rearrange("(a p) n -> p a n", a=QT)),
        (bqc[:], bq_cols), (blkc[:], blk_cols), (bkc[:], bk_col_d),
        (gq_sb[:], qn_col_d), (gk_sb[:], kn_col_d),
        (gql[:], qn_lo_d), (gqm[:], qn_mid_d), (gkl[:], kn_lo_d), (gkm[:], kn_mid_d),
        (msc[0:1, 6:7], bc_d),
        (scr2[0:16, 512:512 + HD], sink_k_d),
        (rows_bf[0:1, 0:D], blv_row_d),
        (rows_bf[0:1, D:2 * D], bo_row_d),
        (rows_bf[0:1, 2 * D:3 * D], sv_row_d),
        (rows_bf[0:1, 3 * D:3 * D + HD], bv_row_d),
    ]:
        p.dma(SY, dst, src, "sq")
    P0_SQ = p.val("sq")

    p.wait(V, "sq", P0_SQ)
    p.op(V, lambda v: v.memset(halfn[:], -0.5))
    p.op(V, lambda v: v.memset(halfp[:], 0.5))
    p.op(V, lambda v: v.memset(msc[:], 0.0))
    p.op(V, lambda v: v.memset(col1[:], 0.0))
    p.op(V, lambda v: v.memset(mcol[:], 0.0))
    p.op(V, lambda v: v.memset(stat[:], 0.0), sig="v")
    for i in range(4):
        p.wait(SY, "t", p.val("t"))
        p.dma(SY, scr2[0:1, 0:512], Wc_row[:, 512 * i:512 * (i + 1)], "sq")
        wq_t = p.val("sq")
        p.wait(V, "sq", wq_t)
        p.op(V, lambda v: v.reduce_sum(msc[0:1, 1:2], scr2[0:1, 0:512], axis=AX.X))
        p.op(V, lambda v: v.tensor_add(msc[0:1, 0:1], msc[0:1, 0:1], msc[0:1, 1:2]),
             sig="v")
        p.wait(T, "v", p.val("v"))
        p.op(T, lambda t, i=i: t.matmul(PS[0][:, 0:512], onesr[:], scr2[0:1, 0:512],
                                        start=True, stop=True), sig="t")
        p.wait(V, "t")
        p.op(V, lambda v, i=i: v.tensor_copy(wcb[:, 512 * i:512 * (i + 1)], PS[0][:]),
             sig="v")
    p.op(V, lambda v: v.tensor_scalar(msc[0:1, 1:2], msc[0:1, 0:1], -1.0, None,
                                      ALU.mult, ALU.bypass), sig="v")
    p.wait(T, "v")
    p.op(T, lambda t: t.matmul(PS[0][:, 0:1], onesr[:], msc[0:1, 1:2],
                               start=True, stop=True))
    p.op(T, lambda t: t.matmul(PS[0][:, 1:2], onesr[:], msc[0:1, 6:7],
                               start=True, stop=True), sig="t")
    p.wait(V, "t")
    p.op(V, lambda v: v.tensor_copy(wcb[:, D:D + 1], PS[0][:, 0:1]))
    p.op(V, lambda v: v.tensor_copy(mcol[:, 1:2], PS[0][:, 1:2]), sig="v")
    p.wait(T, "sq", P0_SQ)
    p.op(T, lambda t: t.transpose(PS[1][0:128, 0:16], scr2[0:16, 512:512 + HD],
                                  id_f[0:16, 0:16]), sig="t")
    p.wait(V, "t")
    p.op(V, lambda v: v.tensor_copy(sinkT[:], PS[1][0:128, 0:16]), sig="v")
    P0_V = p.val("v")

    w_mats = [(wg_d, FF), (wu_d, FF), (wd_d, D)]
    pass1_jobs = [(mi, w, MROW, t) for mi, (w, MROW) in enumerate(w_mats)
                  for t in range(32)]
    p1_i = [0]
    n_l2dma = {}

    def emit_pass1_step():
        i = p1_i[0]
        if i >= len(pass1_jobs):
            return
        p1_i[0] += 1
        mi, w, MROW, t = pass1_jobs[i]
        buf = p1t[i % 2]
        if i >= 2:
            p.wait(SC, "p1c", i - 1)
        ap4 = w.rearrange("(a i) (m j) -> a m i j", i=32, j=128)
        nm = ap4.shape[1]
        na = 128 // nm
        for ai in range(na):
            a_abs = na * t + ai
            p.dma(SC, buf[ai * nm:(ai + 1) * nm].rearrange("p (i j) -> p i j",
                                                           i=32, j=128),
                  ap4[a_abs], "wq")
        tgt = p.val("wq")
        p.wait(V, "wq", tgt)
        for b in range(4):
            dst = s1sb[:, i * 4 + b: i * 4 + b + 1]
            vw = buf.rearrange("p (i j) -> p i j", i=32, j=128)[:, :, 32 * b:32 * b + 32]
            p.op(V, lambda v, dst=dst, vw=vw: v.reduce_sum(
                dst, vw, axis=AX.XY, apply_absolute_value=True),
                sig="p1c" if b == 3 else None)
        if t == 31:
            p.wait(SY, "p1c", p.val("p1c"))
            p.dma(SY, s1_d.rearrange("(p q) -> p q", p=128)[:, mi * 128:(mi + 1) * 128],
                  s1sb[:, mi * 128:(mi + 1) * 128], "s1q")
            s1t = p.val("s1q")
            p.wait(G, "s1q", s1t)
            if MROW == FF:
                sv = s1_d.rearrange("(al B bh mi2 A ah bl) -> mi2 A B ah al bh bl",
                                    al=8, B=2, bh=8, mi2=3, A=8, ah=4, bl=4)[mi]
                dv = l2_d[mi].rearrange("(A B) (ah x) -> A B ah x", A=8, B=2, ah=4)
                for A in range(8):
                    for B in range(2):
                        for ah in range(4):
                            p.dma(G, dv[A, B, ah].rearrange("(al bh bl) -> al bh bl",
                                                            al=8, bh=8, bl=4),
                                  sv[A, B, ah], "l2g")
            else:
                sv = s1_d.rearrange("(ab B bh mi2 A aa bl) -> mi2 A B aa ab bh bl",
                                    ab=2, B=8, bh=8, mi2=3, A=2, aa=16, bl=4)[mi]
                dv = l2_d[mi].rearrange("(A B) (aa x) -> A B aa x", A=2, B=8, aa=16)
                for A in range(2):
                    for B in range(8):
                        for aa in range(16):
                            p.dma(G, dv[A, B, aa].rearrange("(ab bh bl) -> ab bh bl",
                                                            ab=2, bh=8, bl=4),
                                  sv[A, B, aa], "l2g")
            n_l2dma[mi] = p.val("l2g") // 16

    def ln_stats(buf, is_own):
        p.op(V, lambda v: v.tensor_scalar(trash[:, 0:1024], buf[:, 0:1024], 1.0, None,
                                          ALU.mult, ALU.add, accum_out=stat[:, 0:1]))
        p.op(V, lambda v: v.tensor_scalar(trash[:, 0:1024], buf[:, 1024:2048], 1.0, None,
                                          ALU.mult, ALU.add, accum_out=stat[:, 2:3]),
             sig="v")
        p.op(SC, lambda s: s.activation(trash[:, 0:1024], buf[:, 0:1024], ACT.Square,
                                        accum_out=stat[:, 4:5]))
        p.op(SC, lambda s: s.activation(trash[:, 0:1024], buf[:, 1024:2048], ACT.Square,
                                        accum_out=stat[:, 5:6]), sig="s")
        p.wait(V, "s")
        p.op(V, lambda v: v.tensor_add(stat[:, 0:1], stat[:, 0:1], stat[:, 2:3]))
        p.op(V, lambda v: v.tensor_add(stat[:, 4:5], stat[:, 4:5], stat[:, 5:6]))
        p.op(V, lambda v: v.tensor_scalar(stat[:, 1:2], stat[:, 0:1], 1.0 / D, None,
                                          ALU.mult, ALU.bypass))
        p.op(V, lambda v: v.tensor_scalar(stat[:, 7:8], stat[:, 4:5], 1.0 / D, None,
                                          ALU.mult, ALU.bypass))
        p.op(V, lambda v: v.tensor_tensor(stat[:, 8:9], stat[:, 1:2], stat[:, 1:2],
                                          ALU.mult))
        p.op(V, lambda v: v.tensor_scalar(stat[:, 8:9], stat[:, 8:9], -1.0, 1e-5,
                                          ALU.mult, ALU.add))
        p.op(V, lambda v: v.tensor_add(stat[:, 9:10], stat[:, 7:8], stat[:, 8:9]),
             sig="v")
        p.wait(SC, "v")
        p.op(SC, lambda s: s.activation(stat[:, 10:11], stat[:, 9:10], ACT.Sqrt),
             sig="s")
        p.wait(V, "s")
        p.op(V, lambda v: v.reciprocal(stat[:, 11:12], stat[:, 10:11]))
        if is_own:
            p.op(V, lambda v: v.tensor_tensor(stat[:, 12:13], stat[:, 1:2],
                                              stat[:, 11:12], ALU.mult))
            p.op(V, lambda v: v.tensor_scalar(stat[:, 12:13], stat[:, 12:13], -1.0,
                                              None, ALU.mult, ALU.bypass))
        p.op(V, lambda v: None, sig="v")

    own_done_s = [0]

    def emit_entries_tile(j):
        buf = xt_sb[j % 2]
        if j >= 2:
            p.wait(SY, "entc", j - 1)
        if j % 16 < 2 and j >= 16:
            p.wait(SY, "ownc", p.val("ownc"))
            p.wait(SY, "s", own_done_s[0])
        p.dma(SY, buf[:, 0:D], x_full[128 * j:128 * (j + 1)], "xq")
        tgt = p.val("xq")
        p.wait(V, "xq", tgt)
        p.wait(SC, "xq", tgt)
        ln_stats(buf, False)
        p.op(V, lambda v: v.tensor_tensor(scr2[:, 0:1024], buf[:, 0:1024],
                                          wcb[:, 0:1024], ALU.mult))
        p.op(V, lambda v: v.reduce_sum(stat[:, 2:3], scr2[:, 0:1024], axis=AX.X))
        p.op(V, lambda v: v.tensor_tensor(scr2[:, 0:1024], buf[:, 1024:2048],
                                          wcb[:, 1024:2048], ALU.mult))
        p.op(V, lambda v: v.reduce_sum(stat[:, 3:4], scr2[:, 0:1024], axis=AX.X))
        p.op(V, lambda v: v.tensor_add(stat[:, 2:3], stat[:, 2:3], stat[:, 3:4]))
        p.op(V, lambda v: v.tensor_tensor(stat[:, 13:14], stat[:, 1:2],
                                          wcb[:, D:D + 1], ALU.mult))
        p.op(V, lambda v: v.tensor_add(stat[:, 13:14], stat[:, 13:14], stat[:, 2:3]))
        p.op(V, lambda v: v.tensor_scalar(stat[:, 13:14], stat[:, 13:14],
                                          stat[:, 11:12], None, ALU.mult, ALU.bypass))
        p.op(V, lambda v: v.tensor_scalar(stat[:, 13:14], stat[:, 13:14], mcol[:, 1:2],
                                          None, ALU.add, ALU.bypass))
        p.op(V, lambda v: v.tensor_copy(buf[:, D:D + 1], stat[:, 1:2]))
        p.op(V, lambda v: v.tensor_copy(buf[:, D + 1:D + 2], stat[:, 10:11]), sig="v")
        p.wait(T, "v")
        p.op(T, lambda t: t.transpose(PS[4][0:1, 0:128], stat[:, 13:14], id_f[:]),
             sig="t")
        p.wait(V, "t")
        p.op(V, lambda v: v.reduce_max(msc[0:1, 4:5], PS[4][0:1, 0:128], axis=AX.X))
        p.op(V, lambda v: v.tensor_scalar(msc[0:1, 5:6], msc[0:1, 4:5], -1.0, None,
                                          ALU.mult, ALU.bypass), sig="v")
        p.wait(T, "v")
        p.op(T, lambda t: t.matmul(PS[4][:, 4:5], onesr[:], msc[0:1, 5:6],
                                   start=True, stop=True), sig="t")
        p.wait(V, "t")
        p.op(V, lambda v: v.tensor_copy(mcol[:, 0:1], PS[4][:, 4:5]), sig="v")
        p.wait(SC, "v")
        p.op(SC, lambda s: s.activation(col1[:, 0:1], stat[:, 13:14], ACT.Exp,
                                        bias=mcol[:, 0:1], scale=1.0), sig="s")
        p.wait(V, "s")
        p.op(V, lambda v: v.tensor_scalar(col1[:, 1:2], col1[:, 0:1], stat[:, 11:12],
                                          None, ALU.mult, ALU.bypass))
        if j >= 2:
            p.wait(V, "entc", j - 1)
        p.op(V, lambda v, j=j: v.tensor_scalar(acol[j % 2][:], m01[:], col1[:, 1:2],
                                               None, ALU.mult, ALU.bypass), sig="v")
        p.wait(T, "v")
        for fcx in range(FC):
            bank = PS[fcx // 4]
            colo = (fcx % 4) * 128 + (j % 16) * 8
            p.op(T, lambda t, bank=bank, colo=colo, buf=buf, fcx=fcx, j=j: t.matmul(
                bank[:, colo:colo + 8], buf[:, 128 * fcx:128 * (fcx + 1)],
                acol[j % 2][:], start=True, stop=True))
        p.op(T, lambda t, buf=buf, j=j: t.matmul(
            PS[5][0:1, (j % 16) * 8:(j % 16) * 8 + 8], buf[:, D:D + 1],
            acol[j % 2][:], start=True, stop=True))
        p.op(T, lambda t, buf=buf, j=j: t.matmul(
            PS[5][0:1, 128 + (j % 16) * 8:128 + (j % 16) * 8 + 8],
            buf[:, D + 1:D + 2], acol[j % 2][:], start=True, stop=True), sig="entc")
        if j % 16 == 15:
            grp = j // 16
            p.wait(V, "entc", p.val("entc"))
            for fcx in range(FC):
                bank = PS[fcx // 4]
                colo = (fcx % 4) * 128
                p.op(V, lambda v, fcx=fcx, bank=bank, colo=colo, grp=grp: v.tensor_copy(
                    entf[:, fcx * C + grp * 128: fcx * C + grp * 128 + 128],
                    bank[:, colo:colo + 128]))
            p.op(V, lambda v, grp=grp: v.tensor_copy(
                ent_m[0:1, grp * 128:grp * 128 + 128], PS[5][0:1, 0:128]))
            p.op(V, lambda v, grp=grp: v.tensor_copy(
                ent_s[0:1, grp * 128:grp * 128 + 128], PS[5][0:1, 128:256]),
                sig="ventg")

    def emit_own_tile(tc):
        buf = xt_sb[0]
        p.wait(SY, "s", p.val("s"))
        p.wait(SY, "t", p.val("t"))
        p.dma(SY, buf[:, 0:D], x_own[128 * tc:128 * (tc + 1)], "xq")
        tgt = p.val("xq")
        p.wait(V, "xq", tgt)
        p.wait(SC, "xq", tgt)
        ln_stats(buf, True)
        if "own0" in dbg and tc == 0:
            p.wait(SY, "v", p.val("v"))
            p.wait(SY, "s", p.val("s"))
            p.dma(SY, dbg["own0"][:, 0:D], buf[:, 0:D], "dq")
            p.dma(SY, dbg["own0"][:, D:D + 16], stat[:], "dq")
            p.wait(V, "dq", p.val("dq"))
            p.wait(SC, "dq", p.val("dq"))
            p.wait(T, "dq", p.val("dq"))
        if tc >= 1:
            p.op(V, lambda v: v.memset(stat[:, 6:7], 0.0))
            p.op(V, lambda v: v.memset(stat[:, 14:15], 0.0), sig="v")
        for half in range(2):
            p.wait(SC, "v", p.val("v"))
            p.op(SC, lambda s, half=half: s.activation(
                scr2[:, 0:1024], buf[:, 1024 * half:1024 * (half + 1)], ACT.Identity,
                bias=stat[:, 12:13], scale=stat[:, 11:12]), sig="s")
            sxn = p.val("s")
            p.wait(T, "s", sxn)
            for fc8 in range(8):
                fcx = half * 8 + fc8
                pbv = PBf[fc8 % 2]
                if fc8 >= 2:
                    p.wait(T, "v", p.val("v") - 1)
                p.op(T, lambda t, fc8=fc8, pbv=pbv: t.transpose(
                    pbv[:, 0:128], scr2[:, fc8 * 128:fc8 * 128 + 128], id_f[:]),
                    sig="t")
                p.wait(V, "t")
                p.op(V, lambda v, fcx=fcx, tc=tc, pbv=pbv: v.tensor_copy(
                    xnT[:, fcx * NOWN + tc * 128: fcx * NOWN + tc * 128 + 128],
                    pbv[:, 0:128]), sig="v")
            if tc >= 1:
                p.op(SC, lambda s: s.activation(trash[:, 0:1024], scr2[:, 0:1024],
                                                ACT.Square, accum_out=stat[:, 5:6]),
                     sig="s")
                p.wait(V, "s")
                p.op(V, lambda v: v.tensor_add(stat[:, 6:7], stat[:, 6:7],
                                               stat[:, 5:6]))
                p.op(V, lambda v: v.reduce_max(stat[:, 13:14], scr2[:, 0:1024],
                                               axis=AX.X, apply_absolute_value=True))
                p.op(V, lambda v: v.tensor_max(stat[:, 14:15], stat[:, 14:15],
                                               stat[:, 13:14]), sig="v")
        if tc >= 1:
            tq = tc - 1
            p.op(V, lambda v: v.tensor_scalar(stat[:, 5:6], stat[:, 6:7], 1.0 / D, 1e-6,
                                              ALU.mult, ALU.add), sig="v")
            p.wait(SC, "v")
            p.op(SC, lambda s: s.activation(stat[:, 6:7], stat[:, 5:6], ACT.Sqrt),
                 sig="s")
            p.wait(V, "s")
            p.op(V, lambda v: v.reciprocal(stat[:, 7:8], stat[:, 6:7]))
            p.op(V, lambda v: v.tensor_tensor(stat[:, 8:9], stat[:, 14:15],
                                              stat[:, 7:8], ALU.mult))
            p.op(V, lambda v: v.tensor_scalar(stat[:, 8:9], stat[:, 8:9], 1e-5, None,
                                              ALU.max, ALU.bypass))
            p.op(V, lambda v: v.tensor_scalar(stat[:, 9:10], stat[:, 8:9], 1.0 / 127.0,
                                              None, ALU.mult, ALU.bypass))
            p.op(V, lambda v: v.reciprocal(stat[:, 13:14], stat[:, 8:9]))
            p.op(V, lambda v: v.tensor_scalar(stat[:, 13:14], stat[:, 13:14], 127.0,
                                              None, ALU.mult, ALU.bypass))
            p.op(V, lambda v: v.tensor_tensor(stat[:, 13:14], stat[:, 13:14],
                                              stat[:, 7:8], ALU.mult), sig="v")
            p.wait(T, "v")
            p.op(T, lambda t: t.transpose(PS[4][0:1, 0:128], stat[:, 9:10], id_f[:]),
                 sig="t")
            p.wait(V, "t")
            p.op(V, lambda v: v.tensor_copy(rq_bf.bitcast(F32)[0:1, 0:128],
                                            PS[4][0:1, 0:128]), sig="v")
            p.wait(SY, "v", p.val("v"))
            p.dma(SY, recips_d[0:1, tq * 128:tq * 128 + 128],
                  rq_bf.bitcast(F32)[0:1, 0:128], "xq")
            for half in range(2):
                p.wait(SC, "v", p.val("v"))
                p.op(SC, lambda s, half=half: s.activation(
                    scr2[:, 0:1024], buf[:, 1024 * half:1024 * (half + 1)],
                    ACT.Identity, bias=stat[:, 12:13], scale=stat[:, 11:12]), sig="s")
                p.wait(V, "s")
                p.op(V, lambda v: v.tensor_scalar(scr2[:, 0:1024], scr2[:, 0:1024],
                                                  stat[:, 13:14], MAGIC, ALU.mult,
                                                  ALU.add))
                p.op(V, lambda v: v.tensor_scalar(scr2[:, 0:1024], scr2[:, 0:1024],
                                                  MAGIC, None, ALU.subtract,
                                                  ALU.bypass), sig="v")
                p.wait(T, "v")
                for fc8 in range(8):
                    fcx = half * 8 + fc8
                    pbv = PBf[fc8 % 2]
                    if fc8 >= 2:
                        p.wait(T, "v", p.val("v") - 1)
                    p.op(T, lambda t, fc8=fc8, pbv=pbv: t.transpose(
                        pbv[:, 0:128], scr2[:, fc8 * 128:fc8 * 128 + 128], id_f[:]),
                        sig="t")
                    p.wait(V, "t")
                    p.op(V, lambda v, fcx=fcx, pbv=pbv: v.tensor_copy(
                        xqstage[:, fcx * 128:fcx * 128 + 128], pbv[:, 0:128]), sig="v")
            p.wait(SY, "v", p.val("v"))
            p.dma(SY, xq_d.rearrange("(c p) t -> p c t", p=128)
                  [:, :, tq * 128:tq * 128 + 128],
                  xqstage.rearrange("p (c t) -> p c t", c=FC), "xq")
            if "xq" in dbg and tq == 0:
                p.dma(SY, dbg["xq"], xqstage[:], "dq")
        own_done_s[0] = p.val("s")
        p.op(SY, lambda s: None, sig="ownc")

    p.wait(V, "v", P0_V)
    for grp in range(4):
        for j in range(grp * 16, grp * 16 + 16):
            emit_entries_tile(j)
            if j % 2 == 0:
                emit_pass1_step()
        for tc in range(grp * 2, grp * 2 + 2):
            p.wait(SY, "entc", p.val("entc"))
            emit_own_tile(tc)
    p.wait(SY, "entc", p.val("entc"))
    emit_own_tile(8)
    if "xnT" in dbg:
        p.wait(SY, "v", p.val("v"))
        p.dma(SY, dbg["xnT"], xnT, "dq")
    if "entT" in dbg:
        p.wait(SY, "ventg", p.val("ventg"))
        p.dma(SY, dbg["entT"], entf, "dq")

    p.wait(V, "ventg", p.val("ventg"))
    if "entT" in dbg:
        p.wait(V, "dq", p.val("dq"))
    p.op(V, lambda v: v.reciprocal(rowB[0:1, 0:512], ent_s[0:1, :]))
    p.op(V, lambda v: v.tensor_tensor(ent_m[0:1, :], ent_m[0:1, :],
                                      rowB[0:1, 0:512], ALU.mult), sig="v")
    p.wait(T, "v", p.val("v"))
    for ci in range(4):
        p.op(T, lambda t, ci=ci: t.matmul(PS[0][:, ci * 128:(ci + 1) * 128], onesr[:],
                                          ent_m[0:1, ci * 128:(ci + 1) * 128],
                                          start=True, stop=True))
        p.op(T, lambda t, ci=ci: t.matmul(PS[1][:, ci * 128:(ci + 1) * 128], onesr[:],
                                          rowB[0:1, ci * 128:(ci + 1) * 128],
                                          start=True, stop=True),
             sig="t" if ci == 3 else None)
    p.wait(V, "t")
    for fcx in range(FC):
        for ci in range(4):
            src = entf[:, fcx * C + ci * 128: fcx * C + ci * 128 + 128]
            dst = entTn[:, fcx * C + ci * 128: fcx * C + ci * 128 + 128]
            pslice = slice(ci * 128, (ci + 1) * 128)
            p.op(V, lambda v, src=src, pslice=pslice: v.tensor_tensor(
                scr2[:, 0:128], src, PS[1][:, pslice], ALU.mult))
            p.op(V, lambda v, dst=dst, pslice=pslice: v.tensor_sub(
                dst, scr2[:, 0:128], PS[0][:, pslice]),
                sig="v" if (fcx == FC - 1 and ci == 3) else None)
    P2_ENT_V = p.val("v")
    p.wait(G, "entc", p.val("entc"))
    p.wait(G, "ownc", p.val("ownc"))
    p.dma(G, wk_sb.rearrange("p (c d) -> p c d", c=FC), Wk_d.rearrange("(c p) d -> p c d", p=128), "gq")
    p.dma(G, wv_sb.rearrange("p (c d) -> p c d", c=FC), Wv_d.rearrange("(c p) d -> p c d", p=128), "gq")
    GQ_WKV = p.val("gq")

    p.wait(T, "gq", GQ_WKV)
    p.wait(T, "v", P2_ENT_V)
    for ci in range(4):
        for kc in range(FC):
            p.op(T, lambda t, ci=ci, kc=kc: t.matmul(
                PS[2][:, 0:128], wk_sb[:, kc * 128:(kc + 1) * 128],
                entTn[:, kc * C + ci * 128: kc * C + ci * 128 + 128],
                start=(kc == 0), stop=(kc == FC - 1)),
                sig="t" if kc == FC - 1 else None)
        p.wait(V, "t")
        p.op(V, lambda v, ci=ci: v.tensor_scalar(ckf[:, ci * 128:(ci + 1) * 128],
                                                 PS[2][:, 0:128], bkc[:], None,
                                                 ALU.add, ALU.bypass), sig="v")
        p.wait(T, "v")
    p.wait(SC, "v", p.val("v"))
    p.op(SC, lambda s: s.activation(qsq[:, 0:512], ckf[:], ACT.Square), sig="s")
    p.wait(T, "s")
    p.op(T, lambda t: t.matmul(PS[2][0:1, 0:512], onescb[:], qsq[:, 0:512],
                               start=True, stop=True), sig="t")
    p.wait(V, "t")
    p.op(V, lambda v: v.tensor_scalar(rowB[0:1, 0:512], PS[2][0:1, 0:512], 1.0 / HD,
                                      1e-6, ALU.mult, ALU.add), sig="v")
    p.wait(SC, "v")
    p.op(SC, lambda s: s.activation(rowB[0:1, 512:1024], rowB[0:1, 0:512], ACT.Sqrt),
         sig="s")
    p.wait(V, "s")
    p.op(V, lambda v: v.reciprocal(rowB[0:1, 0:512], rowB[0:1, 512:1024]))
    p.op(V, lambda v: v.tensor_copy(rq_bf[0:1, 0:512], rowB[0:1, 0:512]), sig="v")
    p.op(V, lambda v: v.scalar_tensor_tensor(scr2[0:32, 0:512], ckf[0:32, :],
                                             gk_sb[0:32, 0:1], coskb[0:32, :],
                                             ALU.mult, ALU.mult))
    p.op(V, lambda v: v.scalar_tensor_tensor(scr2[0:32, 512:1024], ckf[32:64, :],
                                             gk_sb[32:64, 0:1], sinkb[32:64, :],
                                             ALU.mult, ALU.mult))
    p.op(V, lambda v: v.scalar_tensor_tensor(scr2[32:64, 0:512], ckf[0:32, :],
                                             gk_sb[0:32, 0:1], sinkb[0:32, :],
                                             ALU.mult, ALU.mult))
    p.op(V, lambda v: v.scalar_tensor_tensor(scr2[32:64, 512:1024], ckf[32:64, :],
                                             gk_sb[32:64, 0:1], coskb[32:64, :],
                                             ALU.mult, ALU.mult))
    p.op(V, lambda v: v.tensor_sub(ckf[0:32, :], scr2[0:32, 0:512],
                                   scr2[0:32, 512:1024]))
    p.op(V, lambda v: v.tensor_add(ckf[32:64, :], scr2[32:64, 0:512],
                                   scr2[32:64, 512:1024]))
    p.op(V, lambda v: v.tensor_scalar(ckf[64:128, :], ckf[64:128, :],
                                      gk_sb[64:128, 0:1], None, ALU.mult, ALU.bypass),
         sig="v")
    p.wait(T, "v", p.val("v"))
    p.op(T, lambda t: t.matmul(PS[2][:, 0:512], onesbr[:], rq_bf[0:1, 0:512],
                               start=True, stop=True), sig="t")
    p.wait(V, "t")
    p.op(V, lambda v: v.tensor_tensor(ckT[:], ckf[:], PS[2][:, 0:512], ALU.mult),
         sig="v")
    p.wait(T, "v", p.val("v"))
    for cc in range(4):
        p.wait(T, "s", p.val("s"))
        for kc in range(FC):
            p.op(T, lambda t, cc=cc, kc=kc: t.matmul(
                PS[3][:, 0:128], entTn[:, kc * C + cc * 128: kc * C + cc * 128 + 128],
                wv_sb[:, kc * 128:(kc + 1) * 128], start=(kc == 0), stop=False))
        p.op(T, lambda t: t.matmul(PS[3][:, 0:128], onesbr[:],
                                   rows_bf[0:1, 3 * D:3 * D + HD],
                                   start=False, stop=True), sig="t")
        p.wait(SC, "t")
        p.op(SC, lambda s, cc=cc: s.copy(cv_sb[:, cc * 128:(cc + 1) * 128],
                                         PS[3][:, 0:128]), sig="s")
    P2_S = p.val("s")
    P2_V = p.val("v")
    if "ckT" in dbg:
        p.wait(SY, "v", P2_V)
        p.dma(SY, dbg["ckT"], ckT, "dq")
    if "cv" in dbg:
        p.wait(SY, "s", P2_S)
        p.dma(SY, dbg["cv"], cv_sb, "dq")

    def proj_epilogue(is_q, h):
        n_all = TOK if is_q else NOWN
        dstb = q_sb if is_q else k_sb
        bcols = bqc if is_q else blkc
        gall = gq_sb if is_q else gk_sb
        glo, gmi = gall[0:32, 0:1], gall[32:64, 0:1]
        cosS = cosb[:, HALO:HALO + TOK] if is_q else cosb[:, 0:NOWN]
        sinS = sinb[:, HALO:HALO + TOK] if is_q else sinb[:, 0:NOWN]
        banks = [PS[0], PS[1]] if is_q else [PS[0], PS[1], PS[2]]
        widths = [512, 512] if is_q else [512, 512, 128]
        bdst = bcq if is_q else bck
        p.wait(SC, "t", p.val("t"))
        for nh in range(len(banks)):
            n0, nw = nh * 512, widths[nh]
            p.op(SC, lambda s, bank=banks[nh], n0=n0, nw=nw: s.activation(
                qsq[:, n0:n0 + nw], bank[:, 0:nw], ACT.Square,
                bias=bcols[:, h:h + 1], scale=1.0), sig="s")
        sqs = p.val("s")
        for nh in range(len(banks)):
            n0, nw = nh * 512, widths[nh]
            p.wait(T, "s", sqs)
            p.wait(T, "bcv", p.val("bcv"))
            p.op(T, lambda t, n0=n0, nw=nw: t.matmul(
                PS[3][0:1, 0:nw], onescb[:], qsq[:, n0:n0 + nw],
                start=True, stop=True), sig="t")
            p.wait(V, "t")
            p.op(V, lambda v, nw=nw: v.tensor_scalar(rowA[0:1, 0:nw],
                                                     PS[3][0:1, 0:nw], 1.0 / HD, 1e-6,
                                                     ALU.mult, ALU.add), sig="v")
            p.wait(SC, "v")
            p.op(SC, lambda s, nw=nw: s.activation(rowA[0:1, 0:nw], rowA[0:1, 0:nw],
                                                   ACT.Sqrt), sig="s")
            p.wait(V, "s")
            p.op(V, lambda v, nw=nw: v.reciprocal(rowA[0:1, 0:nw], rowA[0:1, 0:nw]))
            if is_q:
                p.op(V, lambda v, nw=nw: v.tensor_scalar(rowA[0:1, 0:nw],
                                                         rowA[0:1, 0:nw], SCL, None,
                                                         ALU.mult, ALU.bypass))
            p.op(V, lambda v, n0=n0, nw=nw: v.tensor_copy(rq_bf[0:1, n0:n0 + nw],
                                                          rowA[0:1, 0:nw]), sig="v")
            p.wait(T, "v")
            p.op(T, lambda t, n0=n0, nw=nw: t.matmul(
                PS[3][:, 0:nw], onesbr[:], rq_bf[0:1, n0:n0 + nw],
                start=True, stop=True), sig="t")
            p.wait(V, "t")
            p.op(V, lambda v, bdst=bdst, n0=n0, nw=nw: v.tensor_copy(
                bdst[:, n0:n0 + nw], PS[3][:, 0:nw]), sig="bcv")
        bl = bcols[0:32, h:h + 1]
        bm = bcols[32:64, h:h + 1]
        br = bcols[64:128, h:h + 1]
        for nh in range(len(banks)):
            n0, nw = nh * 512, widths[nh]
            bank = banks[nh]
            p.wait(V, "bcv", p.val("bcv"))
            p.op(V, lambda v, bank=bank, nw=nw, bl=bl, glo=glo: v.tensor_scalar(
                scr2[0:32, 0:nw], bank[0:32, 0:nw], bl, glo, ALU.add, ALU.mult))
            p.op(V, lambda v, bank=bank, nw=nw, bm=bm, gmi=gmi: v.tensor_scalar(
                scr2[32:64, 0:nw], bank[32:64, 0:nw], bm, gmi, ALU.add, ALU.mult))
            p.op(V, lambda v, n0=n0, nw=nw, cosS=cosS: v.tensor_tensor(
                scr2[64:96, 0:nw], scr2[0:32, 0:nw], cosS[0:32, n0:n0 + nw],
                ALU.mult))
            p.op(V, lambda v, n0=n0, nw=nw, sinS=sinS: v.tensor_tensor(
                scr2[64:96, 512:512 + nw], scr2[32:64, 0:nw],
                sinS[32:64, n0:n0 + nw], ALU.mult))
            p.op(V, lambda v, nw=nw: v.tensor_sub(scr2[0:32, 512:512 + nw],
                                                  scr2[64:96, 0:nw],
                                                  scr2[64:96, 512:512 + nw]))
            p.op(V, lambda v, n0=n0, nw=nw, sinS=sinS: v.tensor_tensor(
                scr2[96:128, 0:nw], scr2[0:32, 0:nw], sinS[0:32, n0:n0 + nw],
                ALU.mult))
            p.op(V, lambda v, n0=n0, nw=nw, cosS=cosS: v.tensor_tensor(
                scr2[96:128, 512:512 + nw], scr2[32:64, 0:nw],
                cosS[32:64, n0:n0 + nw], ALU.mult))
            p.op(V, lambda v, nw=nw: v.tensor_add(scr2[32:64, 512:512 + nw],
                                                  scr2[96:128, 0:nw],
                                                  scr2[96:128, 512:512 + nw]))
            p.op(V, lambda v, bank=bank, nw=nw, br=br, gall=gall: v.tensor_scalar(
                scr2[64:128, 512:512 + nw], bank[64:128, 0:nw], br,
                gall[64:128, 0:1], ALU.add, ALU.mult))
            p.op(V, lambda v, dstb=dstb, bdst=bdst, n0=n0, nw=nw: v.tensor_tensor(
                dstb[:, n0:n0 + nw], scr2[:, 512:512 + nw], bdst[:, n0:n0 + nw],
                ALU.mult), sig="v")

    vq_done = {"v": P2_V, "s": P2_S}
    head_t_hist = []
    for h in range(NH):
        wq_b, wk_b, wv_b = wcol[(3 * h) % 6], wcol[(3 * h + 1) % 6], wcol[(3 * h + 2) % 6]
        if h >= 2:
            p.wait(G, "t", head_t_hist[h - 2])
        if h == 0:
            p.wait(G, "s", P2_S)
        for wbuf, wsrc in [(wq_b, Wq_d), (wk_b, Wlk_d), (wv_b, Wlv_d)]:
            p.dma(G, wbuf.rearrange("p (c d) -> p c d", c=FC),
                  wsrc[:, 128 * h:128 * (h + 1)]
                  .rearrange("(c p) d -> p c d", p=128), "gq")
        gq_t = p.val("gq")
        p.wait(T, "gq", gq_t)
        p.wait(T, "v", vq_done["v"])
        p.wait(T, "s", vq_done["s"])
        for nh in range(2):
            bank = [PS[0], PS[1]][nh]
            for kc in range(FC):
                p.op(T, lambda t, nh=nh, kc=kc, bank=bank, wq_b=wq_b: t.matmul(
                    bank[:], wq_b[:, kc * 128:(kc + 1) * 128],
                    xnT[:, kc * NOWN + HALO + nh * 512: kc * NOWN + HALO + nh * 512 + 512],
                    start=(kc == 0), stop=(kc == FC - 1)),
                    sig="t" if kc == FC - 1 else None)
        proj_epilogue(True, h)
        vq = p.val("v")
        p.wait(T, "v", vq)
        for nh in range(3):
            bank = [PS[0], PS[1], PS[2]][nh]
            nw = [512, 512, 128][nh]
            for kc in range(FC):
                p.op(T, lambda t, nh=nh, kc=kc, bank=bank, nw=nw, wk_b=wk_b: t.matmul(
                    bank[:, 0:nw], wk_b[:, kc * 128:(kc + 1) * 128],
                    xnT[:, kc * NOWN + nh * 512: kc * NOWN + nh * 512 + nw],
                    start=(kc == 0), stop=(kc == FC - 1)),
                    sig="t" if kc == FC - 1 else None)
        proj_epilogue(False, h)
        vk = p.val("v")
        p.wait(T, "v", vk)
        vh_buf = v_h[h % 2]
        for tc9 in range(9):
            bank = [PS[4], PS[5]][tc9 % 2]
            if tc9 >= 2:
                p.wait(T, "s", p.val("s") - 1)
            for kc in range(FC):
                p.op(T, lambda t, tc9=tc9, kc=kc, bank=bank, wv_b=wv_b: t.matmul(
                    bank[:, 0:128],
                    xnT[:, kc * NOWN + tc9 * 128: kc * NOWN + tc9 * 128 + 128],
                    wv_b[:, kc * 128:(kc + 1) * 128], start=(kc == 0), stop=False))
            p.op(T, lambda t, bank=bank, h=h: t.matmul(
                bank[:, 0:128], onesbr[:], rows_bf[0:1, 128 * h:128 * h + 128],
                start=False, stop=True), sig="t")
            p.wait(SC, "t")
            p.op(SC, lambda s, tc9=tc9, bank=bank, vh_buf=vh_buf: s.copy(
                vh_buf[:, tc9 * 128:(tc9 + 1) * 128], bank[:, 0:128]), sig="s")
        vs = p.val("s")
        for j in range(QT):
            qt = q_sb[:, j * 128:(j + 1) * 128]
            p.wait(T, "s", vs)
            if j >= 1:
                p.wait(T, "v", p.val("v"))
            p.op(T, lambda t, qt=qt, j=j: t.matmul(
                PS[0][:, 0:256], qt, k_sb[:, j * 128:j * 128 + 256],
                start=True, stop=True))
            p.op(T, lambda t, qt=qt, h=h: t.matmul(
                PS[0][:, 256:257], qt, sinkT[:, h:h + 1], start=True, stop=True))
            p.op(T, lambda t, qt=qt: t.matmul(
                PS[1][:], qt, ckT[:], start=True, stop=True), sig="t")
            tsc = p.val("t")
            p.wait(V, "t", tsc)
            mloc = m_loc[:, 256:512] if j == 0 else m_loc[:, 0:256]
            p.op(V, lambda v, mloc=mloc: v.tensor_add(attL[:], PS[0][:, 0:256], mloc))
            p.op(V, lambda v: v.reduce_max(col1[:, 0:1], attL[:], axis=AX.X))
            p.op(V, lambda v: v.tensor_scalar(col1[:, 1:2], col1[:, 0:1], -1.0, None,
                                              ALU.mult, ALU.bypass), sig="v")
            p.wait(SC, "v")
            p.op(SC, lambda s: s.activation(attL[:], attL[:], ACT.Exp,
                                            bias=col1[:, 1:2], scale=1.0,
                                            accum_out=col1[:, 2:3]), sig="s")
            p.wait(V, "s")
            p.op(V, lambda v: v.reciprocal(col1[:, 3:4], col1[:, 2:3]))
            p.op(V, lambda v: v.tensor_scalar(attLb[:], attL[:], col1[:, 3:4], 0.5,
                                              ALU.mult, ALU.mult), sig="v")
            p.op(V, lambda v, j=j: v.tensor_add(attC[:, 0:512], PS[1][:],
                                                m_cmp[:, j * C:(j + 1) * C]))
            p.op(V, lambda v: v.tensor_copy(attC[:, 512:513], PS[0][:, 256:257]))
            p.op(V, lambda v: v.reduce_max(col1[:, 4:5], attC[:, 0:513], axis=AX.X))
            p.op(V, lambda v: v.tensor_scalar(col1[:, 5:6], col1[:, 4:5], -1.0, None,
                                              ALU.mult, ALU.bypass), sig="v")
            p.wait(SC, "v")
            p.op(SC, lambda s: s.activation(attC[:, 0:513], attC[:, 0:513], ACT.Exp,
                                            bias=col1[:, 5:6], scale=1.0,
                                            accum_out=col1[:, 6:7]), sig="s")
            p.wait(V, "s")
            p.op(V, lambda v: v.reciprocal(col1[:, 7:8], col1[:, 6:7]))
            p.op(V, lambda v: v.tensor_scalar(attCb[:, 0:513], attC[:, 0:513],
                                              col1[:, 7:8], 0.5, ALU.mult, ALU.mult),
                 sig="v")
            vatt = p.val("v")
            p.wait(T, "v", vatt)
            for tchunk in range(2):
                p.op(T, lambda t, tchunk=tchunk: t.transpose(
                    PB[0][:, tchunk * 128:tchunk * 128 + 128],
                    attLb[:, tchunk * 128:tchunk * 128 + 128], id_bf[:]))
            for cchunk in range(4):
                p.op(T, lambda t, cchunk=cchunk: t.transpose(
                    PB[0][:, 256 + cchunk * 128:256 + cchunk * 128 + 128],
                    attCb[:, cchunk * 128:cchunk * 128 + 128], id_bf[:]))
            p.op(T, lambda t: t.transpose(PB[1][0:1, 0:128], attCb[:, 512:513],
                                          id_bf[:]), sig="t")
            ttr = p.val("t")
            p.wait(V, "t", ttr)
            p.op(V, lambda v: v.tensor_copy(attT[:, 0:768], PB[0][:, 0:768]))
            p.op(V, lambda v: v.tensor_copy(srow_bf[0:1, :], PB[1][0:1, 0:128]),
                 sig="v")
            vtr = p.val("v")
            p.wait(T, "v", vtr)
            p.wait(T, "s", p.val("s"))
            p.op(T, lambda t, j=j, vh_buf=vh_buf: t.matmul(
                PS[2][:, 0:128], vh_buf[:, j * 128:j * 128 + 128], attT[:, 0:128],
                start=True, stop=False))
            p.op(T, lambda t, j=j, vh_buf=vh_buf: t.matmul(
                PS[2][:, 0:128], vh_buf[:, (j + 1) * 128:(j + 1) * 128 + 128],
                attT[:, 128:256], start=False, stop=False))
            for cchunk in range(4):
                p.op(T, lambda t, cchunk=cchunk: t.matmul(
                    PS[2][:, 0:128], cv_sb[:, cchunk * 128:(cchunk + 1) * 128],
                    attT[:, 256 + cchunk * 128:256 + cchunk * 128 + 128],
                    start=False, stop=False))
            p.op(T, lambda t, h=h: t.matmul(
                PS[2][:, 0:128], rows_bf[0:1, 2 * D + h * 128:2 * D + h * 128 + 128],
                srow_bf[0:1, :], start=False, stop=True), sig="t")
            p.wait(SC, "t")
            p.op(SC, lambda s, h=h, j=j: s.copy(
                mergedT[:, h * TOK + j * 128: h * TOK + j * 128 + 128],
                PS[2][:, 0:128]), sig="s")
            if (h * QT + j) % 4 == 3:
                emit_pass1_step()
        vq_done = {"v": p.val("v"), "s": p.val("s")}
        head_t_hist.append(p.val("t"))

    P3_S = p.val("s")
    if "mergedT" in dbg:
        p.wait(SY, "s", P3_S)
        p.dma(SY, dbg["mergedT"], mergedT, "dq")
    if "qk" in dbg:
        p.wait(SY, "s", P3_S)
        p.dma(SY, dbg["qk"][:, 0:TOK], q_sb, "dq")
        p.dma(SY, dbg["qk"][:, TOK:TOK + NOWN], k_sb, "dq")

    evac_hist = []
    p4_t_hist = []
    for ncq in range(4):
        if ncq == 0:
            p.wait(G, "s", P3_S)
        else:
            p.wait(G, "t", p4_t_hist[-1])
        p.dma(G, wo_buf.rearrange("p (c n) -> p c n", c=FC),
              Wo_d[:, 512 * ncq:512 * (ncq + 1)]
              .rearrange("(c p) n -> p c n", p=128), "gq")
        gq_t = p.val("gq")
        p.wait(T, "gq", gq_t)
        if ncq == 0:
            p.wait(T, "s", P3_S)
        for tt in range(8):
            bank = PS[tt % 4]
            idx = ncq * 8 + tt
            if idx >= 2:
                p.wait(SY, "p4v", evac_hist[idx - 2])
            p.dma(SY, xres[tt % 2][:], x_own[HALO + tt * 128: HALO + (tt + 1) * 128,
                                             512 * ncq:512 * (ncq + 1)], "rq")
            xq_t = p.val("rq")
            if idx >= 4:
                p.wait(T, "p4v", evac_hist[idx - 4])
            for kc in range(FC):
                p.op(T, lambda t, kc=kc, tt=tt, bank=bank: t.matmul(
                    bank[:], mergedT[:, kc * TOK + tt * 128: kc * TOK + tt * 128 + 128],
                    wo_buf[:, kc * 512:(kc + 1) * 512], start=(kc == 0), stop=False))
            p.op(T, lambda t, ncq=ncq, bank=bank: t.matmul(
                bank[:], onesbr[:], rows_bf[0:1, D + 512 * ncq: D + 512 * (ncq + 1)],
                start=False, stop=True), sig="t")
            p.wait(V, "t")
            p.wait(V, "rq", xq_t)
            ev = p.op(V, lambda v, tt=tt, bank=bank: v.tensor_add(
                res_sb[tt % 2][:], bank[:], xres[tt % 2][:]), sig="p4v")
            evac_hist.append(ev)
            p.wait(SY, "p4v", p.val("p4v"))
            p.dma(SY, resid_d[tt * 128:(tt + 1) * 128, 512 * ncq:512 * (ncq + 1)],
                  res_sb[tt % 2][:], "rq")
        p4_t_hist.append(p.val("t"))
        emit_pass1_step(); emit_pass1_step()
    while p1_i[0] < len(pass1_jobs):
        emit_pass1_step()
    if "resid" in dbg:
        p.wait(SY, "rq", p.val("rq"))
        p.dma(SY, dbg["resid"], resid_d, "dq")

    if "s1d" in dbg:
        p.wait(SY, "l2g", n_l2dma[2] * 16)
        p.dma(SY, dbg["s1d"], s1_d.rearrange("(p q) -> p q", p=128), "dq")
        p.dma(SY, dbg["l2d"], l2_d.rearrange("m a q -> (m a) q"), "dq")
    for mi in range(3):
        p.wait(SY, "l2g", n_l2dma[mi] * 16)
        p.wait(SY, "v", p.val("v"))
        p.dma(SY, l2buf[:], l2_d[mi], "fq")
        g_t = p.val("fq")
        p.wait(V, "fq", g_t)
        p.op(V, lambda v: v.reduce_sum(col1[0:16, 2:3], l2buf, axis=AX.X), sig="v")
        p.wait(T, "v", p.val("v"))
        p.wait(T, "t", p.val("t"))
        p.op(T, lambda t: t.transpose(PS[4][0:1, 0:16], col1[0:16, 2:3], id_f[0:16, 0:16]),
             sig="t")
        p.wait(V, "t")
        p.op(V, lambda v: v.tensor_copy(rowA[0:1, 0:16], PS[4][0:1, 0:16]), sig="v")
        p.op(V, lambda v: v.reduce_sum(msc[0:1, 8:9], rowA[0:1, 0:8], axis=AX.X))
        p.op(V, lambda v: v.reduce_sum(msc[0:1, 9:10], rowA[0:1, 8:16], axis=AX.X))
        p.op(V, lambda v: v.tensor_add(msc[0:1, 8:9], msc[0:1, 8:9], msc[0:1, 9:10]))
        p.op(V, lambda v, mi=mi: v.tensor_scalar(msc[0:1, 10 + mi:11 + mi],
                                                 msc[0:1, 8:9], 5.96046448e-08, 1e-5,
                                                 ALU.mult, ALU.max), sig="v")
    p.op(V, lambda v: v.reciprocal(msc[0:1, 13:16], msc[0:1, 10:13]), sig="v")
    p.wait(T, "v", p.val("v"))
    p.op(T, lambda t: t.matmul(PS[4][:, 0:3], onesr[:], msc[0:1, 13:16],
                               start=True, stop=True))
    p.op(T, lambda t: t.matmul(PS[4][:, 3:6], onesr[:], msc[0:1, 10:13],
                               start=True, stop=True), sig="t")
    p.wait(V, "t")
    p.op(V, lambda v: v.tensor_copy(mcol[:, 2:5], PS[4][:, 0:3]))
    p.op(V, lambda v: v.tensor_copy(mcol[:, 5:8], PS[4][:, 3:6]), sig="v")
    if "wmean" in dbg:
        p.wait(SY, "v", p.val("v"))
        p.dma(SY, dbg["wmean"], msc[:], "dq")
    p.wait(SY, "v", p.val("v"))
    p.wait(SY, "xq", p.val("xq"))
    p.dma(SY, shared_rows[0:1, 0:1024], recips_d, "fq")
    rec_t = p.val("fq")
    if "recips" in dbg:
        p.wait(SY, "fq", rec_t)
        p.dma(SY, dbg["recips"][:, 0:TOK], shared_rows, "dq")
    p.wait(T, "v", p.val("v"))
    p.wait(T, "fq", rec_t)
    for half in range(2):
        p.wait(T, "v", p.val("v"))
        p.op(T, lambda t, half=half: t.matmul(
            PS[4][:], onesr[:], shared_rows[0:1, half * 512:(half + 1) * 512],
            start=True, stop=True), sig="t")
        p.wait(V, "t")
        p.op(V, lambda v, half=half: v.tensor_scalar(
            bgu[:, half * 512:(half + 1) * 512], PS[4][:], mcol[:, 5:6], 0.5,
            ALU.mult, ALU.mult))
        p.op(V, lambda v, half=half: v.tensor_scalar(
            bgu[:, TOK + half * 512:TOK + (half + 1) * 512], PS[4][:], mcol[:, 6:7],
            0.5, ALU.mult, ALU.mult), sig="v")
    P5PRE_V = p.val("v")

    tern_s_hist = []

    def emit_tern_slab(widx, sl, src_ap, store_d, dst3d=False):
        wn = wnat[sl % 2]
        sb_out = slab_sb[1 - (sl % 2)]
        te = tern_sb[sl % 2]
        if sl >= 2 and tern_s_hist:
            p.wait(SY, "s", tern_s_hist[-2] if len(tern_s_hist) >= 2 else 0)
        p.dma(SY, wn.rearrange("p (c f) -> p c f", c=FC) if dst3d else wn[:],
              src_ap, "fq")
        f_t = p.val("fq")
        guard_t = p.val("t")
        guard_rq = p.val("rq")
        scol = mcol[:, 2 + widx:3 + widx]
        for half in range(2):
            hs = slice(1024 * half, 1024 * (half + 1))
            p.wait(SC, "fq", f_t)
            if half == 0:
                p.wait(SC, "t", guard_t)
                p.wait(SC, "rq", guard_rq)
            if sl >= 1 or half == 1:
                p.wait(SC, "v", p.val("v"))
            p.op(SC, lambda s, wn=wn, hs=hs, scol=scol, te=te: s.activation(
                te[:, hs], wn[:, hs], ACT.Sign, bias=halfn[:], scale=scol))
            p.op(SC, lambda s, wn=wn, hs=hs, scol=scol: s.activation(
                tscr[:, 0:1024], wn[:, hs], ACT.Sign, bias=halfp[:], scale=scol), sig="s")
            p.wait(V, "s")
            p.op(V, lambda v, te=te, hs=hs: v.tensor_add(te[:, hs], te[:, hs],
                                                         tscr[:, 0:1024]), sig="v")
        vt = p.val("v")
        p.wait(T, "v", vt)
        for dc in range(FC):
            pb = PB[dc % 2]
            if dc >= 2:
                p.wait(T, "s", p.val("s") - 1)
            p.op(T, lambda t, te=te, dc=dc, pb=pb: t.transpose(
                pb[:, 0:128], te[:, dc * 128:(dc + 1) * 128], id_bf[:]), sig="t")
            p.wait(SC, "t")
            p.op(SC, lambda s, dc=dc, pb=pb, sb_out=sb_out: s.copy(
                sb_out[:, dc * 128:(dc + 1) * 128], pb[:, 0:128]), sig="s")
        st = p.val("s")
        if store_d is not None:
            p.wait(SY, "s", st)
            p.dma(SY, store_d[128 * sl:128 * (sl + 1)], sb_out[:], "rq")
        tern_s_hist.append(p.val("s"))
        return sb_out, st

    p.wait(SY, "v", P5PRE_V)
    for sl in range(NSLAB):
        srcv = wd_d[:, 128 * sl:128 * (sl + 1)].rearrange("(dc p) f -> p dc f", p=128)
        emit_tern_slab(2, sl, srcv, slabd_d, dst3d=True)

    if "slabd" in dbg:
        p.wait(SY, "rq", p.val("rq"))
        p.dma(SY, dbg["slabd"], slabd_d, "dq")

    p.wait(SY, "xq", p.val("xq"))
    p.dma(SY, xqa.rearrange("p (c t) -> p c t", c=FC),
          xq_d.rearrange("(c p) t -> p c t", p=128)[:, :, 0:TOKA], "fq")
    xqa_t = p.val("fq")
    p.op(V, lambda v: v.memset(hmax[:, 0:TOK], 0.0), sig="v")

    def gateup_passA(widx, w_d, store_d, psA, psB):
        for sl in range(NSLAB):
            sb_out, st = emit_tern_slab(widx, sl, w_d[128 * sl:128 * (sl + 1)],
                                        store_d)
            p.wait(T, "s", st)
            p.wait(T, "fq", xqa_t)
            p.wait(T, "guv", p.val("guv"))
            for ti, (t0, tw) in enumerate([(0, 512), (512, 256)]):
                bank = [psA, psB][ti]
                for kc in range(FC):
                    p.op(T, lambda t, kc=kc, t0=t0, tw=tw, bank=bank, sb_out=sb_out:
                         t.matmul(bank[:, 0:tw], sb_out[:, kc * 128:(kc + 1) * 128],
                                  xqa[:, kc * TOKA + t0: kc * TOKA + t0 + tw],
                                  start=(kc == 0), stop=(kc == FC - 1)),
                         sig="t" if (kc == FC - 1 and ti == 1) else None)
            tg = p.val("t")
            p.wait(V, "t", tg)
            if widx == 0:
                for ti, (t0, tw) in enumerate([(0, 512), (512, 256)]):
                    bank = [psA, psB][ti]
                    p.op(V, lambda v, bank=bank, tw=tw, t0=t0, sl=sl: v.tensor_tensor(
                        h_sb[:, sl * TOKA + t0: sl * TOKA + t0 + tw], bank[:, 0:tw],
                        bgu[:, t0:t0 + tw], ALU.mult),
                        sig="guv" if ti == 1 else None)
            else:
                for ti, (t0, tw) in enumerate([(0, 512), (512, 256)]):
                    bank = [psA, psB][ti]
                    p.op(V, lambda v, bank=bank, tw=tw, t0=t0: v.tensor_tensor(
                        bxh[:, t0:t0 + tw], bank[:, 0:tw],
                        bgu[:, TOK + t0:TOK + t0 + tw], ALU.mult),
                        sig="guv" if ti == 1 else None)
                p.wait(SC, "guv", p.val("guv"))
                p.op(SC, lambda s, sl=sl: s.activation(
                    scr2[:, 0:TOKA], h_sb[:, sl * TOKA:(sl + 1) * TOKA], ACT.Silu),
                    sig="s")
                p.wait(V, "s")
                p.op(V, lambda v, sl=sl: v.tensor_tensor(
                    h_sb[:, sl * TOKA:(sl + 1) * TOKA], scr2[:, 0:TOKA],
                    bxh[:, 0:TOKA], ALU.mult), sig="v")
                p.wait(SC, "v", p.val("v"))
                p.op(SC, lambda s, sl=sl: s.activation(
                    qsq[:, 0:TOKA], h_sb[:, sl * TOKA:(sl + 1) * TOKA], ACT.Square),
                    sig="s")
                p.wait(V, "s", p.val("s"))
                p.op(V, lambda v: v.tensor_tensor(
                    hmax[:, 0:TOKA], hmax[:, 0:TOKA], qsq[:, 0:TOKA], ALU.max),
                    sig="v")
                p.wait(T, "s", p.val("s"))
                p.op(T, lambda t, sl=sl: t.matmul(
                    PS[2][0:1, 0:512], onescb[:], qsq[:, 0:512],
                    start=(sl == 0), stop=(sl == NSLAB - 1)))
                p.op(T, lambda t, sl=sl: t.matmul(
                    PS[3][0:1, 0:256], onescb[:], qsq[:, 512:768],
                    start=(sl == 0), stop=(sl == NSLAB - 1)),
                    sig="t" if sl == NSLAB - 1 else None)

    gateup_passA(0, wg_d, slabg_d, PS[0], PS[1])
    gateup_passA(1, wu_d, slabu_d, PS[0], PS[1])
    if "slabg" in dbg:
        p.wait(SY, "rq", p.val("rq"))
        p.dma(SY, dbg["slabg"], slabg_d, "dq")
    if "hsb" in dbg:
        p.wait(SY, "v", p.val("v"))
        p.dma(SY, dbg["hsb"], h_sb, "dq")
        p.wait(V, "dq", p.val("dq"))

    def hquant(tokn, hbuf, TW, ssq_banks):
        p.wait(V, "t", p.val("t"))
        nchunks = [(0, min(512, tokn))] + ([(512, tokn - 512)] if tokn > 512 else [])
        for ci, (o0, ow) in enumerate(nchunks):
            p.op(V, lambda v, o0=o0, ow=ow, b=ssq_banks[ci]: v.tensor_scalar(
                rowB[0:1, o0:o0 + ow], b[0:1, 0:ow], 1.0 / FF, 1e-6,
                ALU.mult, ALU.add), sig="v")
        if "hdiag" in dbg and tokn == TOKA:
            p.wait(SY, "v", p.val("v"))
            p.dma(SY, dbg["hdiag"][0:1, :], rowB[0:1, 0:TOKA], "dq")
            p.dma(SY, dbg["hmaxd"], hmax, "dq")
            p.wait(V, "dq", p.val("dq"))
        nt = tokn // 128
        p.wait(T, "v", p.val("v"))
        for ti in range(nt):
            p.wait(T, "v", p.val("v"))
            p.op(T, lambda t, ti=ti: t.transpose(
                PS[4][:, 0:128], hmax[:, ti * 128:ti * 128 + 128], id_f[:]), sig="t")
            p.wait(V, "t")
            p.op(V, lambda v: v.reduce_max(col1[:, 6:7], PS[4][:, 0:128], axis=AX.X),
                 sig="v")
            p.wait(T, "v", p.val("v"))
            p.op(T, lambda t: t.transpose(PS[4][0:1, 0:128], col1[:, 6:7], id_f[:]),
                 sig="t")
            p.wait(V, "t")
            p.op(V, lambda v, ti=ti: v.tensor_copy(
                rsh_row[0:1, ti * 128:ti * 128 + 128], PS[4][0:1, 0:128]), sig="v")
        p.wait(SC, "v", p.val("v"))
        p.op(SC, lambda s, tokn=tokn: s.activation(
            rsh_row[0:1, 0:tokn], rsh_row[0:1, 0:tokn], ACT.Sqrt), sig="s")
        p.wait(V, "s", p.val("s"))
        if "hdiag" in dbg and tokn == TOKA:
            p.wait(SY, "s", p.val("s"))
            p.dma(SY, dbg["hdiag"][1:2, :], rsh_row[0:1, 0:TOKA], "dq")
            p.wait(V, "dq", p.val("dq"))
            p.wait(SC, "dq", p.val("dq"))
        for o0, ow in nchunks:
            p.wait(SC, "v", p.val("v"))
            p.op(SC, lambda s, o0=o0, ow=ow: s.activation(
                rowA[0:1, 0:ow], rowB[0:1, o0:o0 + ow], ACT.Sqrt), sig="s")
            p.wait(V, "s")
            p.op(V, lambda v, o0=o0, ow=ow: v.reciprocal(rowB[0:1, o0:o0 + ow],
                                                         rowA[0:1, 0:ow]))
            p.op(V, lambda v, o0=o0, ow=ow: v.tensor_tensor(
                rsh_row[0:1, o0:o0 + ow], rsh_row[0:1, o0:o0 + ow],
                rowB[0:1, o0:o0 + ow], ALU.mult))
            p.op(V, lambda v, o0=o0, ow=ow: v.tensor_scalar(
                rsh_row[0:1, o0:o0 + ow], rsh_row[0:1, o0:o0 + ow], 1e-5, None,
                ALU.max, ALU.bypass))
            p.op(V, lambda v, o0=o0, ow=ow: v.reciprocal(scr2[0:1, 0:ow],
                                                         rsh_row[0:1, o0:o0 + ow]))
            p.op(V, lambda v, o0=o0, ow=ow: v.tensor_scalar(
                scr2[0:1, 0:ow], scr2[0:1, 0:ow], 127.0, None, ALU.mult, ALU.bypass))
            p.op(V, lambda v, o0=o0, ow=ow: v.tensor_tensor(
                scr2[0:1, 0:ow], scr2[0:1, 0:ow], rowB[0:1, o0:o0 + ow], ALU.mult))
            p.op(V, lambda v, o0=o0, ow=ow: v.tensor_copy(rq_bf[0:1, o0:o0 + ow],
                                                          scr2[0:1, 0:ow]))
            p.op(V, lambda v, o0=o0, ow=ow: v.tensor_scalar(
                rsh_row[0:1, o0:o0 + ow], rsh_row[0:1, o0:o0 + ow], 1.0 / 127.0, None,
                ALU.mult, ALU.bypass))
            p.op(V, lambda v, o0=o0, ow=ow: v.tensor_scalar(
                rsh_row[0:1, o0:o0 + ow], rsh_row[0:1, o0:o0 + ow],
                msc[0:1, 12:13], 0.5, ALU.mult, ALU.mult), sig="v")
        p.wait(T, "v", p.val("v"))
        for ti in range(nt):
            p.wait(T, "v", p.val("v"))
            p.op(T, lambda t, ti=ti: t.matmul(
                PS[4][:, 0:1], rsh_row[0:1, ti * 128:ti * 128 + 128],
                onesr[0:1, 0:1], start=True, stop=True), sig="t")
            p.wait(V, "t")
            p.op(V, lambda v, ti=ti: v.tensor_copy(col1[:, ti:ti + 1],
                                                   PS[4][:, 0:1]), sig="v")
        p.wait(T, "v", p.val("v"))
        for o0, ow in nchunks:
            p.wait(T, "v", p.val("v"))
            p.op(T, lambda t, o0=o0, ow=ow: t.matmul(
                PS[4][:, 0:ow], onesbr[:], rq_bf[0:1, o0:o0 + ow],
                start=True, stop=True), sig="t")
            p.wait(V, "t")
            p.op(V, lambda v, o0=o0, ow=ow: v.tensor_copy(bxh[:, o0:o0 + ow],
                                                          PS[4][:, 0:ow]), sig="v")
        for sl in range(NSLAB):
            p.op(V, lambda v, sl=sl, hbuf=hbuf, TW=TW, tokn=tokn: v.tensor_tensor(
                scr2[:, 0:tokn], hbuf[:, sl * TW: sl * TW + tokn], bxh[:, 0:tokn],
                ALU.mult))
            p.op(V, lambda v, tokn=tokn: v.tensor_scalar(
                scr2[:, 0:tokn], scr2[:, 0:tokn], MAGIC, MAGIC, ALU.add, ALU.subtract))
            p.op(V, lambda v, sl=sl, hbuf=hbuf, TW=TW, tokn=tokn: v.tensor_copy(
                hbuf[:, sl * TW: sl * TW + tokn], scr2[:, 0:tokn]), sig="v")

    hquant(TOKA, h_sb, TOKA, [PS[2], PS[3]])
    HQA_V = p.val("v")
    if "hq" in dbg:
        p.wait(SY, "v", HQA_V)
        p.dma(SY, dbg["hq"], h_sb, "dq")
        p.dma(SY, dbg["hrow"], rsh_row[0:1, 0:TOKA], "dq")
        p.dma(SY, dbg["hsig"], rq_bf[0:1, 0:TOKA], "dq")

    banks8 = [PS[0], PS[1], PS[2], PS[3], PS[4], PS[5], PBf[0], PBf[1]]

    def down_pass(tlist, hbuf, TW, hq_v):
        slab_t_hist = []
        for sl in range(NSLAB):
            buf = slab_sb[sl % 2]
            if sl >= 2:
                p.wait(SY, "t", slab_t_hist[sl - 2])
            elif sl == 0:
                p.wait(SY, "v", hq_v)
                p.wait(SY, "rq", p.val("rq"))
                p.wait(SY, "t", p.val("t"))
            p.dma(SY, buf[:], slabd_d[128 * sl:128 * (sl + 1)], "fq")
            f_t = p.val("fq")
            p.wait(T, "fq", f_t)
            if sl == 0:
                p.wait(T, "v", hq_v)
                p.wait(T, "dev", p.val("dev"))
            for ti, tc in enumerate(tlist):
                for nq in range(4):
                    bank = banks8[ti * 4 + nq]
                    p.op(T, lambda t, tc=tc, nq=nq, sl=sl, bank=bank, buf=buf,
                         hbuf=hbuf, TW=TW: t.matmul(
                             bank[:, 0:512],
                             hbuf[:, sl * TW + tc * 128: sl * TW + tc * 128 + 128],
                             buf[:, nq * 512:(nq + 1) * 512],
                             start=(sl == 0), stop=(sl == NSLAB - 1)),
                         sig="t" if (ti == len(tlist) - 1 and nq == 3) else None)
            slab_t_hist.append(p.val("t"))
        return p.val("t")

    oq_hist = []

    def down_evac(abs_t0, tlist, tg):
        p.wait(V, "t", tg)
        for ti, tc in enumerate(tlist):
            rc = col1[:, tc:tc + 1]
            for nq in range(4):
                bank = banks8[ti * 4 + nq]
                tt_abs = abs_t0 + tc * 128
                p.dma(SY, xres[nq % 2][:],
                      resid_d[tt_abs:tt_abs + 128, 512 * nq:512 * (nq + 1)], "rq2")
                r_t = p.val("rq2")
                p.wait(V, "rq2", r_t)
                p.op(V, lambda v, bank=bank, rc=rc: v.tensor_scalar(
                    scr2[:, 0:512], bank[:, 0:512], rc, None, ALU.mult, ALU.bypass))
                if len(oq_hist) >= 2:
                    p.wait(V, "oq", oq_hist[-2])
                p.op(V, lambda v, nq=nq: v.tensor_add(outt[nq % 2][:], scr2[:, 0:512],
                                                      xres[nq % 2][:]), sig="dev")
                p.wait(SY, "dev", p.val("dev"))
                p.dma(SY, out_d[tt_abs:tt_abs + 128, 512 * nq:512 * (nq + 1)],
                      outt[nq % 2][:], "oq")
                oq_hist.append(p.val("oq"))

    for sub in range(3):
        tg = down_pass([2 * sub, 2 * sub + 1], h_sb, TOKA, HQA_V)
        down_evac(0, [2 * sub, 2 * sub + 1], tg)

    p.wait(SY, "dev", p.val("dev"))
    p.dma(SY, xqb.rearrange("p (c t) -> p c t", c=FC),
          xq_d.rearrange("(c p) t -> p c t", p=128)[:, :, TOKA:TOK], "fq")
    xqb_t = p.val("fq")
    p.op(V, lambda v: v.memset(hmax[:, 0:TOKB], 0.0), sig="v")

    def gateup_passB(widx, slab_d):
        slab_t_hist = []
        for sl in range(NSLAB):
            buf = slab_sb[sl % 2]
            if sl >= 2:
                p.wait(SY, "t", slab_t_hist[sl - 2])
            elif sl == 0:
                p.wait(SY, "t", p.val("t"))
            p.dma(SY, buf[:], slab_d[128 * sl:128 * (sl + 1)], "fq")
            f_t = p.val("fq")
            p.wait(T, "fq", max(f_t, xqb_t))
            p.wait(T, "guv", p.val("guv"))
            for kc in range(FC):
                p.op(T, lambda t, kc=kc, buf=buf: t.matmul(
                    PS[0][:, 0:TOKB], buf[:, kc * 128:(kc + 1) * 128],
                    xqb[:, kc * TOKB:(kc + 1) * TOKB],
                    start=(kc == 0), stop=(kc == FC - 1)),
                    sig="t" if kc == FC - 1 else None)
            tg = p.val("t")
            p.wait(V, "t", tg)
            if widx == 0:
                p.op(V, lambda v, sl=sl: v.tensor_tensor(
                    hb_sb[:, sl * TOKB:(sl + 1) * TOKB], PS[0][:, 0:TOKB],
                    bgu[:, TOKA:TOK], ALU.mult), sig="guv")
            else:
                p.op(V, lambda v: v.tensor_tensor(
                    bxh[:, 0:TOKB], PS[0][:, 0:TOKB], bgu[:, TOK + TOKA:2 * TOK],
                    ALU.mult), sig="guv")
                p.wait(SC, "guv", p.val("guv"))
                p.op(SC, lambda s, sl=sl: s.activation(
                    scr2[:, 0:TOKB], hb_sb[:, sl * TOKB:(sl + 1) * TOKB], ACT.Silu),
                    sig="s")
                p.wait(V, "s")
                p.op(V, lambda v, sl=sl: v.tensor_tensor(
                    hb_sb[:, sl * TOKB:(sl + 1) * TOKB], scr2[:, 0:TOKB],
                    bxh[:, 0:TOKB], ALU.mult), sig="v")
                p.wait(SC, "v", p.val("v"))
                p.op(SC, lambda s, sl=sl: s.activation(
                    qsq[:, 0:TOKB], hb_sb[:, sl * TOKB:(sl + 1) * TOKB], ACT.Square),
                    sig="s")
                p.wait(V, "s", p.val("s"))
                p.op(V, lambda v: v.tensor_tensor(
                    hmax[:, 0:TOKB], hmax[:, 0:TOKB], qsq[:, 0:TOKB], ALU.max),
                    sig="v")
                p.wait(T, "s", p.val("s"))
                p.op(T, lambda t, sl=sl: t.matmul(
                    PS[1][0:1, 0:TOKB], onescb[:], qsq[:, 0:TOKB],
                    start=(sl == 0), stop=(sl == NSLAB - 1)),
                    sig="t" if sl == NSLAB - 1 else None)
            slab_t_hist.append(p.val("t"))

    gateup_passB(0, slabg_d)
    gateup_passB(1, slabu_d)
    hquant(TOKB, hb_sb, TOKB, [PS[1]])
    HQB_V = p.val("v")
    tg = down_pass([0, 1], hb_sb, TOKB, HQB_V)
    down_evac(TOKA, [0, 1], tg)

    p.wait(SY, "oq", p.val("oq"))
    for e in (V, SC, T, G):
        p.wait(e, "oq", p.val("oq"))

    with nc.Block() as block:
        p.emit(block)
    if debug:
        print("hazard vnops inserted:", p.hz_stats[0])
    return nc, dbg


def _host_inputs(core, I):
    x = np.ascontiguousarray(np.asarray(I["L0_attn_hidden_states"],
                                        np.float32).reshape(S, D))
    t0 = core * TOK
    x_ownh = np.zeros((NOWN, D), np.float32)
    lo = max(t0 - HALO, 0)
    x_ownh[HALO - (t0 - lo):] = x[lo:t0 + TOK]
    qn = np.asarray(I["L0_attn_qn_w"], np.float32)
    kn = np.asarray(I["L0_attn_kn_w"], np.float32)

    pos_own = np.arange(t0 - HALO, t0 + TOK)
    half = ROPE // 2
    inv_freq = (1.0 / 10000.0 ** (np.arange(half, dtype=np.float32) * 2.0 / ROPE)
                ).astype(np.float32)
    ang = pos_own.astype(np.float32)[None, :] * inv_freq[:, None]
    block_ends = (np.arange(C) * R + (R - 1)).astype(np.float32)
    angb = block_ends[None, :] * inv_freq[:, None]

    iidx = np.arange(256)[None, :]
    pidx = np.arange(128)[:, None]
    valid_g = (iidx >= pidx) & (iidx <= pidx + WIN)
    mk_g = np.where(valid_g, np.float32(0.0), np.float32(-3.0e38)).astype(BF2)
    valid_f = valid_g & (iidx >= (HALO if core == 0 else 0))
    mk_f = np.where(valid_f, np.float32(0.0), np.float32(-3.0e38)).astype(BF2)
    mask_loc = np.concatenate([mk_g, mk_f], 0)

    mask_cmp = np.zeros((QT, 128, C), np.float32)
    for j in range(QT):
        nvis = np.minimum((t0 + j * 128 + np.arange(128) + 1) // 16, C)
        cols = np.arange(C)[None, :]
        mask_cmp[j] = np.where(cols < nvis[:, None], 0.0, -3.0e38)
    mask_cmp = mask_cmp.reshape(QT * 128, C).astype(BF2)

    m01h = np.zeros((128, 8), np.float32)
    for pp in range(128):
        m01h[pp, pp // 16] = 1.0

    d = {
        "x_full": x,
        "x_own": x_ownh,
        "Wq": I["L0_attn_Wq"], "Wlk": I["L0_attn_Wlk"], "Wlv": I["L0_attn_Wlv"],
        "Wo": I["L0_attn_Wo"], "Wk": I["L0_attn_Wk"], "Wv": I["L0_attn_Wv"],
        "Wc_row": np.asarray(I["L0_attn_Wc"], np.float32).reshape(1, D),
        "bq_cols": np.asarray(I["L0_attn_bq"], np.float32).reshape(FC, 128).T,
        "blk_cols": np.asarray(I["L0_attn_blk"], np.float32).reshape(FC, 128).T,
        "blv_row": np.asarray(I["L0_attn_blv"], np.float32).reshape(1, D).astype(BF2),
        "bo_row": np.asarray(I["L0_attn_bo"], np.float32).reshape(1, D).astype(BF2),
        "bv_row": np.asarray(I["L0_attn_bv"], np.float32).reshape(1, HD).astype(BF2),
        "bc": np.asarray(I["L0_attn_bc"], np.float32).reshape(1, 1),
        "bk_col": np.asarray(I["L0_attn_bk"], np.float32).reshape(HD, 1),
        "qn_col": qn.reshape(HD, 1), "kn_col": kn.reshape(HD, 1),
        "qn_lo": qn[0:32].reshape(32, 1), "qn_mid": qn[32:64].reshape(32, 1),
        "kn_lo": kn[0:32].reshape(32, 1), "kn_mid": kn[32:64].reshape(32, 1),
        "sink_k": I["L0_attn_sink_k"],
        "sv_row": np.asarray(I["L0_attn_sink_v"], np.float32)
        .reshape(1, NH * HD).astype(BF2),
        "wg": I["L0_ffn_w_gate"], "wu": I["L0_ffn_w_up"], "wd": I["L0_ffn_w_down"],
        "ident_bf": np.eye(128, dtype=np.float32).astype(BF2),
        "ident_f": np.eye(128, dtype=np.float32),
        "ones_row": np.ones((1, 128), np.float32),
        "ones_bfrow": np.ones((1, 128), np.float32).astype(BF2),
        "onescol_bf": np.ones((128, 1), np.float32).astype(BF2),
        "m01_in": m01h,
        "cos_own": np.concatenate([np.cos(ang), np.cos(ang)], 0).astype(np.float32),
        "sin_own": np.concatenate([np.sin(ang), np.sin(ang)], 0).astype(np.float32),
        "cos_blk": np.concatenate([np.cos(angb), np.cos(angb)], 0).astype(np.float32),
        "sin_blk": np.concatenate([np.sin(angb), np.sin(angb)], 0).astype(np.float32),
        "mask_loc": mask_loc,
        "mask_cmp": mask_cmp,
    }
    return {k: np.ascontiguousarray(np.asarray(v)) for k, v in d.items()}


_CACHE = {}


def kernel(**inputs):
    if "nc" not in _CACHE:
        _CACHE["nc"] = build_program()
    nc, dbg = _CACHE["nc"]
    in_maps = [_host_inputs(core, inputs) for core in range(NCORE)]
    res = run_bass_kernel_spmd(nc, in_maps, core_ids=list(range(NCORE)))
    out = np.concatenate([np.asarray(r["out"], np.float32) for r in res.results], 0)
    return out.reshape(1, S, D)
